# version 03
# baseline (speedup 1.0000x reference)
import numpy as np
import ml_dtypes
from contextlib import ExitStack

import concourse.bass as bass
import concourse.mybir as mybir
from concourse.bass_utils import run_bass_kernel_spmd

F32 = mybir.dt.float32
BF16 = mybir.dt.bfloat16
AX = mybir.AxisListType.X
OP = mybir.AluOpType
AF = mybir.ActivationFunctionType
BF = ml_dtypes.bfloat16
MAGIC = float(np.float32(12582912.0))
D = 2048
NCH = D // 128
SCALE_L1 = float(1.0 / np.sqrt(128.0))


def _banned_tile(kc, g):
    j = 128 * kc + np.arange(128)[:, None]
    i = 128 * g + np.arange(128)[None, :]
    return (j >= 4) & (j < i - 512)


def _mask_info(S):
    NB = S // 128
    NQB = (S // 2) // 512
    skip = {qb: frozenset(kc for kc in range(1, NB) if kc <= 8 * qb - 5)
            for qb in range(NQB)}
    midx, order = {}, []
    for qb in range(NQB):
        for kc in range(NB):
            if kc in skip[qb]:
                continue
            if any(_banned_tile(kc, 2 * (4 * qb + sb) + p).any()
                   for p in (0, 1) for sb in range(4)):
                midx[(qb, kc)] = len(order)
                order.append((qb, kc))
    return skip, midx, order


def _masks_for_parity(S, p, order):
    n = max(len(order), 1)
    m = np.ones((n, 128, 512), dtype=BF)
    for idx, (qb, kc) in enumerate(order):
        for sb in range(4):
            g = 2 * (4 * qb + sb) + p
            m[idx][:, 128 * sb:128 * (sb + 1)] = (~_banned_tile(kc, g)).astype(BF)
    return m


class Sem:
    def __init__(self, h):
        self.h = h
        self.n = 0

    def inc(self, instr, k=1):
        instr.then_inc(self.h, k)
        self.n += k
        return instr


def build(S, dbg_x1=False):
    T = S // 2
    NT = T // 128
    NTB = T // 512
    NQB = NTB
    NB = S // 128
    skip, midx, morder = _mask_info(S)
    NMASK = max(len(morder), 1)
    MAXM = max((sum(1 for (q, _) in morder if q == qb) for qb in range(NQB)),
               default=1)

    nc = bass.Bass("TRN2", target_bir_lowering=False, num_devices=8)

    xin = nc.dram_tensor("xin", [NT, 128, D], F32, kind="ExternalInput").ap()
    wq0 = nc.dram_tensor("wq0", [D, D], F32, kind="ExternalInput").ap()
    wk0 = nc.dram_tensor("wk0", [D, D], F32, kind="ExternalInput").ap()
    wv0 = nc.dram_tensor("wv0", [D, D], F32, kind="ExternalInput").ap()
    wo0 = nc.dram_tensor("wo0", [D, D], F32, kind="ExternalInput").ap()
    fw0 = nc.dram_tensor("fw0", [D, D], F32, kind="ExternalInput").ap()
    bo0 = nc.dram_tensor("bo0", [1, D], F32, kind="ExternalInput").ap()
    fb0 = nc.dram_tensor("fb0", [1, D], F32, kind="ExternalInput").ap()
    wq1 = nc.dram_tensor("wq1", [D, D], F32, kind="ExternalInput").ap()
    wk1 = nc.dram_tensor("wk1", [D, D], F32, kind="ExternalInput").ap()
    wv1 = nc.dram_tensor("wv1", [D, D], F32, kind="ExternalInput").ap()
    wo1 = nc.dram_tensor("wo1", [D, D], F32, kind="ExternalInput").ap()
    fw1 = nc.dram_tensor("fw1", [D, D], F32, kind="ExternalInput").ap()
    bq1 = nc.dram_tensor("bq1", [128, NCH], F32, kind="ExternalInput").ap()
    bk1 = nc.dram_tensor("bk1", [128, NCH], F32, kind="ExternalInput").ap()
    bv1 = nc.dram_tensor("bv1", [1, D], F32, kind="ExternalInput").ap()
    bo1 = nc.dram_tensor("bo1", [1, D], F32, kind="ExternalInput").ap()
    gw1 = nc.dram_tensor("gw1", [D, 8], F32, kind="ExternalInput").ap()
    aw1 = nc.dram_tensor("aw1", [D, 128], F32, kind="ExternalInput").ap()
    bw1 = nc.dram_tensor("bw1", [128, D], F32, kind="ExternalInput").ap()
    masks = nc.dram_tensor("masks", [NMASK, 128, 512], BF16, kind="ExternalInput").ap()
    out = nc.dram_tensor("out", [NT, 128, D], F32, kind="ExternalOutput").ap()
    outdbg = None
    if dbg_x1:
        outdbg = nc.dram_tensor("outdbg", [NT, 128, D], F32, kind="ExternalOutput").ap()

    xA = nc.dram_tensor("xA", [NT, 128, D], F32).ap()
    x1d = nc.dram_tensor("x1d", [NT, 128, D], F32).ap()
    xB = nc.dram_tensor("xB", [NT, 128, D], F32).ap()
    wqT_d = nc.dram_tensor("wqT_d", [NCH, 128, D], BF16).ap()
    qkvT_d = nc.dram_tensor("qkvT_d", [NCH, 128, T], BF16).ap()
    qT_d = nc.dram_tensor("qT_d", [NCH, 128, T], BF16).ap()
    kT_own = nc.dram_tensor("kT_own", [NCH, 128, T], BF16)
    v_own = nc.dram_tensor("v_own", [NT, 128, D], BF16)
    kT_all = nc.dram_tensor("kT_all", [2, NCH, 128, T], BF16)
    v_all = nc.dram_tensor("v_all", [2, NT, 128, D], BF16)
    kv_in = [nc.dram_tensor(f"kv_in{g}", [64, 520], F32) for g in range(4)]
    kv_out = [nc.dram_tensor(f"kv_out{g}", [64, 520], F32) for g in range(4)]
    aT_d = nc.dram_tensor("aT_d", [NQB, NCH, 128, 512], BF16).ap()

    ident = nc.inline_tensor(np.eye(128, dtype=BF), name="ident")
    identf = nc.inline_tensor(np.eye(128, dtype=np.float32), name="identf")
    ones_r = nc.inline_tensor(np.ones((1, 128), dtype=BF), name="ones_r")
    ones_c = nc.inline_tensor(np.ones((128, 1), dtype=BF), name="ones_c")
    ones_rf = nc.inline_tensor(np.ones((1, 128), dtype=np.float32), name="ones_rf")
    e16 = np.zeros((8, 128), dtype=np.float32)
    for e in range(8):
        e16[e, 16 * e:16 * e + 16] = 16.0
    expand = nc.inline_tensor(e16.astype(BF), name="expand")

    ctx = ExitStack()
    with ctx:
        def sem(name):
            return Sem(ctx.enter_context(nc.semaphore(name)))

        block = ctx.enter_context(nc.Block())

        pctx = ExitStack()
        ctx.enter_context(pctx)

        def psb(name, shape, dt=F32):
            return pctx.enter_context(nc.sbuf_tensor(name, shape, dt))

        stats = psb("stats", [128, 6 * NT + 2], F32)
        SM, SV, SR, SA, SXS, SRX = 0, NT, 2 * NT, 3 * NT, 4 * NT, 5 * NT
        id_bf = psb("id_bf", [128, 128], BF16)
        id_f = psb("id_f", [128, 128], F32)
        onesr_sb = psb("onesr_sb", [1, 128], BF16)
        onesc_sb = psb("onesc_sb", [128, 1], BF16)
        onesrf_sb = psb("onesrf_sb", [1, 128], F32)
        expand_sb = psb("expand_sb", [8, 128], BF16)
        brow_bf = psb("brow_bf", [1, 4 * D], BF16)
        bqk_sb = psb("bqk_sb", [128, 2 * NCH], F32)
        wsb = psb("wsb", [128, D], F32)

        sCONST = sem("sCONST")
        sBR = sem("sBR")
        brow_ctx = ExitStack()
        brow_sb = brow_ctx.enter_context(
            nc.sbuf_tensor("brow_sb", [1, 4 * D], F32))

        @block.sync
        def _(sync):
            for dst, srcp in [
                (id_bf[:], ident.ap()), (id_f[:], identf.ap()),
                (onesr_sb[:], ones_r.ap()), (onesc_sb[:], ones_c.ap()),
                (onesrf_sb[:], ones_rf.ap()), (expand_sb[:], expand.ap()),
                (brow_sb[:, 0:D], bo0), (brow_sb[:, D:2 * D], fb0),
                (brow_sb[:, 2 * D:3 * D], bv1), (brow_sb[:, 3 * D:4 * D], bo1),
                (bqk_sb[:, 0:NCH], bq1), (bqk_sb[:, NCH:2 * NCH], bk1),
            ]:
                sCONST.inc(sync.dma_start(out=dst, in_=srcp), 16)

        @block.vector
        def _(v):
            v.wait_ge(sCONST.h, sCONST.n)
            v.tensor_tensor(brow_sb[:, 0:D], brow_sb[:, 0:D],
                            brow_sb[:, D:2 * D], OP.add)
            v.drain()
            sBR.inc(v.tensor_copy(brow_bf[:], brow_sb[:]), 1)


        sWQD = [sem("sWQD0"), sem("sWQD1")]
        bVB = [16 * ((NCH + 1) // 2), 16 * (NCH // 2)]
        qs_ = ExitStack()
        with qs_:
            q_stg = [qs_.enter_context(nc.sbuf_tensor(f"q_stg{j}", [128, D], F32)) for j in range(2)]
            q_stg = [t_[:] for t_ in q_stg]
            q_bf = [qs_.enter_context(nc.sbuf_tensor(f"q_bf{j}", [128, D], BF16)) for j in range(2)]
            q_bf = [t_[:] for t_ in q_bf]
            q_tT = [qs_.enter_context(nc.sbuf_tensor(f"q_tT{j}", [128, D], BF16)) for j in range(2)]
            q_tT = [t_[:] for t_ in q_tT]
            q_wsc = qs_.enter_context(nc.sbuf_tensor("q_wsc", [128, NCH], F32))
            q_wst = qs_.enter_context(nc.sbuf_tensor("q_wst", [NCH, 128], F32))
            ws_row = qs_.enter_context(nc.sbuf_tensor("ws_row", [1, D], F32))
            q_tmp = qs_.enter_context(nc.sbuf_tensor("q_tmp", [128, 2], F32))
            q_tps = qs_.enter_context(nc.psum_tensor("q_tps", [NCH, 128], F32))
            q_bp = [qs_.enter_context(nc.psum_tensor(f"q_bp{j}", [128, 128], F32)) for j in range(2)]
            sQL = [sem("sQL0"), sem("sQL1")]
            sQC = sem("sQC")
            sQT = [sem("sQT0"), sem("sQT1")]
            sQS = sem("sQS")
            sPB = [sem("sPB0"), sem("sPB1")]
            sVB = sQL
            sWSR = sem("sWSR")

            @block.sync
            def _(sync):
                def _qtrans(sync, r):
                    sync.wait_ge(sQC.h, r + 1)
                    if r >= 2:
                        sync.wait_ge(sWQD[r % 2].h, 256 * (r // 2))
                    for c in range(NCH):
                        sQT[r % 2].inc(sync.dma_start(
                            out=q_tT[r % 2][:, 128 * c:128 * (c + 1)],
                            in_=q_bf[r % 2][:, 128 * c:128 * (c + 1)],
                            transpose=True), 16)
                for r in range(NCH):
                    if r >= 1:
                        _qtrans(sync, r - 1)
                    if r >= 2:
                        sync.wait_ge(sQC.h, r - 1)
                    sQL[r % 2].inc(sync.dma_start(
                        out=q_stg[r % 2], in_=fw0[128 * r:128 * (r + 1), :]), 16)
                _qtrans(sync, NCH - 1)
                sync.wait_ge(sQS.h, 2)
                sWSR.inc(sync.dma_start(out=ws_row[0:1, :], in_=q_wst[:]), 16)

            @block.gpsimd
            def _(gp):
                for r in range(NCH):
                    gp.wait_ge(sQT[r % 2].h, 16 * NCH * (r // 2 + 1))
                    for c in range(NCH):
                        sWQD[r % 2].inc(gp.dma_start(
                            out=wqT_d[c, :, 128 * r:128 * (r + 1)],
                            in_=q_tT[r % 2][:, 128 * c:128 * (c + 1)]), 16)

            @block.vector
            def _(v):
                for r in range(NCH):
                    v.wait_ge(sQL[r % 2].h, 16 * (r // 2 + 1))
                    if r >= 2:
                        v.wait_ge(sQT[r % 2].h, 16 * NCH * (r // 2))
                    v.reduce_max(q_tmp[:, 0:1], q_stg[r % 2], axis=AX,
                                 apply_absolute_value=True)
                    v.drain()
                    v.tensor_scalar(q_wsc[:, r:r + 1], q_tmp[:, 0:1],
                                    1.0 / 127.0, 1e-8, OP.mult, OP.max)
                    v.drain()
                    v.reciprocal(q_tmp[:, 1:2], q_wsc[:, r:r + 1])
                    v.drain()
                    v.tensor_scalar(q_stg[r % 2], q_stg[r % 2], q_tmp[:, 1:2],
                                    MAGIC, OP.mult, OP.add)
                    v.drain()
                    v.tensor_scalar(q_stg[r % 2], q_stg[r % 2], MAGIC, 127.0,
                                    OP.subtract, OP.min)
                    v.drain()
                    sQC.inc(v.tensor_scalar(q_bf[r % 2], q_stg[r % 2], -128.0,
                                            None, OP.max), 1)
                v.wait_ge(sQS.h, 1)
                _i = v.tensor_copy(q_wst[:], q_tps[:])
                v.drain()
                sQS.inc(_i, 1)
                for r in range(NCH):
                    v.wait_ge(sPB[r % 2].h, r // 2 + 1)
                    _i = v.tensor_copy(wsb[:, 128 * r:128 * (r + 1)], q_bp[r % 2][:])
                    v.drain()
                    sVB[r % 2].inc(_i, 1)

            @block.tensor
            def _(t):
                t.wait_ge(sQC.h, NCH)
                sQS.inc(t.transpose(q_tps[:], q_wsc[:], id_f[:]), 1)
                t.wait_ge(sWSR.h, 16)
                for r in range(NCH):
                    if r >= 2:
                        t.wait_ge(sVB[r % 2].h, bVB[r % 2] + r // 2)
                    sPB[r % 2].inc(t.matmul(q_bp[r % 2][:], onesrf_sb[:],
                             ws_row[0:1, 128 * r:128 * (r + 1)],
                             start=True, stop=True), 1)

        ssL = sem("ssL")
        ssST = sem("ssST")
        ssRS = sem("ssRS")
        ssXN = sem("ssXN")
        ssTR = sem("ssTR")

        def phase_stats_xn(tag, scope, src, xnT_, xqT_, prewait):
            do_q = xqT_ is not None
            ntr = 16 * NCH * (2 if do_q else 1)
            xst = scope.enter_context(nc.sbuf_tensor(f"{tag}_xst", [128, D], F32))[:]
            xst = [xst, xst]
            bn = scope.enter_context(nc.sbuf_tensor(f"{tag}_bn", [128, 26], F32))
            xnf = scope.enter_context(nc.sbuf_tensor(f"{tag}_xnf", [128, D], F32))
            xnb = scope.enter_context(nc.sbuf_tensor(f"{tag}_xnb", [128, D], BF16))
            xqb = scope.enter_context(
                nc.sbuf_tensor(f"{tag}_xqb", [128, D], BF16)) if do_q else None
            sL, sST, sRS, sXN, sTR = ssL, ssST, ssRS, ssXN, ssTR
            bL, bST, bRS, bXN, bTR = sL.n, sST.n, sRS.n, sXN.n, sTR.n

            @block.sync
            def _(sync):
                for m in range(NT):
                    if prewait is not None:
                        for (h, val) in prewait(m):
                            sync.wait_ge(h, val)
                    if m >= 1:
                        sync.wait_ge(sST.h, bST + m)
                    sL.inc(sync.dma_start(out=xst[m % 2], in_=src[m]), 16)
                def _transposes(sync, m):
                    sync.wait_ge(sXN.h, bXN + m + 1)
                    for c in range(NCH):
                        sTR.inc(sync.dma_start(
                            out=xnT_[:, T * c + 128 * m: T * c + 128 * (m + 1)],
                            in_=xnb[:, 128 * c:128 * (c + 1)], transpose=True), 16)
                        if do_q:
                            sTR.inc(sync.dma_start(
                                out=xqT_[:, T * c + 128 * m: T * c + 128 * (m + 1)],
                                in_=xqb[:, 128 * c:128 * (c + 1)], transpose=True), 16)
                for m in range(NT):
                    if m >= 1:
                        _transposes(sync, m - 1)
                    if m == 0:
                        sync.wait_ge(sST.h, bST + NT)
                    else:
                        sync.wait_ge(sXN.h, bXN + m)
                    sL.inc(sync.dma_start(out=xst[m % 2], in_=src[m]), 16)
                _transposes(sync, NT - 1)

            @block.scalar
            def _(s):
                s.wait_ge(sRS.h, bRS + 1)
                sRS.inc(s.activation(stats[:, SR:SR + NT],
                                     stats[:, SR:SR + NT], AF.Sqrt), 1)
                s.drain()

            @block.vector
            def _(v):
                for m in range(NT):
                    v.wait_ge(sL.h, bL + 16 * (m + 1))
                    for i in range(4):
                        v.bn_stats(bn[:, 6 * i:6 * i + 6],
                                   xst[m % 2][:, 512 * i:512 * (i + 1)])
                    v.drain()
                    v.bn_aggr(bn[:, 24:26], bn[:, 0:24])
                    v.drain()
                    v.tensor_copy(stats[:, SM + m:SM + m + 1], bn[:, 24:25])
                    v.drain()
                    sST.inc(v.tensor_copy(stats[:, SV + m:SV + m + 1],
                                          bn[:, 25:26]), 1)
                v.drain()
                _i = v.tensor_scalar(stats[:, SR:SR + NT], stats[:, SV:SV + NT],
                                     1e-5, None, OP.add)
                v.drain()
                sRS.inc(_i, 1)
                v.wait_ge(sRS.h, bRS + 2)
                v.reciprocal(stats[:, SR:SR + NT], stats[:, SR:SR + NT])
                v.drain()
                for m in range(NT):
                    v.wait_ge(sL.h, bL + 16 * (NT + m + 1))
                    if m >= 1:
                        v.wait_ge(sTR.h, bTR + ntr * m)
                    mean = stats[:, SM + m:SM + m + 1]
                    rstd = stats[:, SR + m:SR + m + 1]
                    v.tensor_scalar(xnf[:], xst[m % 2], mean, rstd,
                                    OP.subtract, OP.mult)
                    v.drain()
                    _i = v.tensor_copy(xnb[:], xnf[:])
                    if do_q:
                        absm = stats[:, SA + m:SA + m + 1]
                        xs = stats[:, SXS + m:SXS + m + 1]
                        rxs = stats[:, SRX + m:SRX + m + 1]
                        v.reduce_max(absm, xnf[:], axis=AX,
                                     apply_absolute_value=True)
                        v.drain()
                        v.tensor_scalar(xs, absm, 1.0 / 127.0, 1e-12,
                                        OP.mult, OP.max)
                        v.drain()
                        v.reciprocal(rxs, xs)
                        v.drain()
                        v.tensor_scalar(xnf[:], xnf[:], rxs, MAGIC,
                                        OP.mult, OP.add)
                        v.drain()
                        v.tensor_scalar(xnf[:], xnf[:], MAGIC, 127.0,
                                        OP.subtract, OP.min)
                        v.drain()
                        _i = v.tensor_scalar(xqb[:], xnf[:], -128.0, None, OP.max)
                    v.drain()
                    sXN.inc(_i, 1)

            return bTR, ntr, bXN

        wsD = [sem("wsD0"), sem("wsD1")]
        wsC = sem("wsC")
        wst_i = [0]

        def wstream_dma(tag, scope, blocks, first_wait=(), stg_cols=D):
            stg = [scope.enter_context(
                nc.sbuf_tensor(f"{tag}_stg{j}", [128, stg_cols], F32)) for j in range(2)]
            stg = [t_[:] for t_ in stg]
            i0 = wst_i[0]

            @block.sync
            def _(sync):
                for (h, val) in first_wait:
                    sync.wait_ge(h, val)
                i = i0
                for (srcfn, dst, cols, war) in blocks:
                    for c in range(NCH):
                        if i >= 2:
                            sync.wait_ge(wsC.h, i - 1)
                        wsD[i % 2].inc(sync.dma_start(out=stg[i % 2][:, 0:cols],
                                                      in_=srcfn(c)), 16)
                        i += 1

            wst_i[0] = i0 + NCH * len(blocks)
            return {"stg": stg, "blocks": blocks, "i": i0, "b": 0}

        def wstream_casts(v, st, n=1):
            for _ in range(n):
                (srcfn, dst, cols, war) = st["blocks"][st["b"]]
                if war is not None:
                    war(v)
                for c in range(NCH):
                    i = st["i"]
                    v.wait_ge(wsD[i % 2].h, 16 * (i // 2 + 1))
                    _i = v.tensor_copy(dst[:, cols * c:cols * (c + 1)],
                                       st["stg"][i % 2][:, 0:cols])
                    v.drain()
                    wsC.inc(_i, 1)
                    st["i"] += 1
                st["b"] += 1

        sX1S = [sem("sX1S0"), sem("sX1S1")]
        sFS = [sem("sFS0"), sem("sFS1")]
        sNMU = sem("sNMU")
        NQKV = 32 * NTB

        brow_ctx.close()
        l0o = ctx.enter_context(ExitStack())
        xnT = l0o.enter_context(nc.sbuf_tensor("xnT0", [128, NCH * T], BF16))[:]

        l0m = ExitStack()
        xqT = l0m.enter_context(nc.sbuf_tensor("xqT0", [128, NCH * T], BF16))
        f_wq = l0m.enter_context(nc.sbuf_tensor("f_wq", [128, NCH * 512], BF16))[:]

        QGATE = [(sQT[0].h, sQT[0].n), (sQT[1].h, sQT[1].n),
                 (sWQD[0].h, sWQD[0].n), (sWQD[1].h, sWQD[1].n),
                 (sWSR.h, 16), (sQL[0].h, sQL[0].n),
                 (sQL[1].h, sQL[1].n)]
        bTR0, ntr0, bXN0 = phase_stats_xn("a0", l0m, xin, xnT, xqT,
                                          lambda m: QGATE if m < 2 else [])
        TRTOT0 = bTR0 + ntr0 * NT
        XN0TOT = bXN0 + NT

        f_xi = [l0m.enter_context(nc.sbuf_tensor(f"f_xi{j}", [128, 512], F32)) for j in range(2)]
        f_xi = [t_[:] for t_ in f_xi]
        f_dq = [l0m.enter_context(nc.sbuf_tensor(f"f_dq{j}", [128, 512], F32)) for j in range(2)]
        f_dq = [t_[:] for t_ in f_dq]
        f_xo = [l0m.enter_context(nc.sbuf_tensor(f"f_xo{j}", [128, 512], F32)) for j in range(2)]
        f_xo = [t_[:] for t_ in f_xo]
        f_ps = [l0m.enter_context(nc.psum_tensor(f"f_ps{j}", [128, 512], F32))
                for j in range(2)]
        sFW = sem("sFW")
        sFXI = [sem("sFXI0"), sem("sFXI1")]
        sFM = sem("sFM")
        sFE = sem("sFE")

        @block.scalar
        def _(s):
            for ob in range(4):
                if ob >= 1:
                    s.wait_ge(sFM.h, NT * ob)
                s.wait_ge(sWQD[0].h, 256 * (2 * ob + 2))
                s.wait_ge(sWQD[1].h, 256 * (2 * ob + 2))
                for c in range(NCH):
                    sFW.inc(s.dma_start(
                        out=f_wq[:, 512 * c:512 * (c + 1)],
                        in_=wqT_d[c, :, 512 * ob:512 * (ob + 1)]), 16)

        @block.sync
        def _(sync):
            for ob in range(4):
                for m in range(NT):
                    i = ob * NT + m
                    if i < 2:
                        sync.wait_ge(ssXN.h, XN0TOT)
                        sync.wait_ge(ssTR.h, TRTOT0)
                    else:
                        sync.wait_ge(sFE.h, i - 1)
                    sFXI[i % 2].inc(sync.dma_start(
                        out=f_xi[i % 2],
                        in_=xin[m, :, 512 * ob:512 * (ob + 1)]), 16)

        @block.tensor
        def _(t):
            t.wait_ge(ssTR.h, TRTOT0)
            for ob in range(4):
                t.wait_ge(sFW.h, 256 * (ob + 1))
                for m in range(NT):
                    i = ob * NT + m
                    if i >= 2:
                        t.wait_ge(sFE.h, i - 1)
                    for c in range(NCH):
                        mm = t.matmul(f_ps[i % 2][:],
                                 xqT[:, T * c + 128 * m:T * c + 128 * (m + 1)],
                                 f_wq[:, 512 * c:512 * (c + 1)],
                                 start=(c == 0), stop=(c == NCH - 1))
                    sFM.inc(mm, 1)

        @block.vector
        def _(v):
            v.wait_ge(sQL[0].h, bVB[0] + (NCH + 1) // 2)
            v.wait_ge(sQL[1].h, bVB[1] + NCH // 2)
            for ob in range(4):
                for m in range(NT):
                    i = ob * NT + m
                    v.wait_ge(sFM.h, i + 1)
                    v.wait_ge(sFXI[i % 2].h, 16 * (i // 2 + 1))
                    if i >= 2:
                        v.wait_ge(sFS[i % 2].h, 16 * (i // 2))
                    xs = stats[:, SXS + m:SXS + m + 1]
                    v.tensor_scalar(f_dq[i % 2], f_ps[i % 2][:], xs, None, OP.mult)
                    v.drain()
                    v.tensor_tensor(f_dq[i % 2], f_dq[i % 2],
                                    wsb[:, 512 * ob:512 * (ob + 1)], OP.mult)
                    v.drain()
                    _i = v.tensor_tensor(f_xo[i % 2], f_dq[i % 2], f_xi[i % 2], OP.add)
                    v.drain()
                    sFE.inc(_i, 1)

        @block.gpsimd
        def _(gp):
            for ob in range(4):
                for m in range(NT):
                    i = ob * NT + m
                    gp.wait_ge(sFE.h, i + 1)
                    sFS[i % 2].inc(gp.dma_start(
                        out=xA[m, :, 512 * ob:512 * (ob + 1)],
                        in_=f_xo[i % 2]), 16)

        l0m.close()

        t0s = ExitStack()
        t_wk = t0s.enter_context(nc.sbuf_tensor("t_wk", [128, NCH * 512], BF16))
        t_wv = t0s.enter_context(nc.sbuf_tensor("t_wv", [128, NCH * 512], BF16))
        ktile = t0s.enter_context(nc.sbuf_tensor("ktile", [128, 512], BF16))
        v_sb = t0s.enter_context(nc.sbuf_tensor("v_sb", [128, 520], BF16))
        qg = t0s.enter_context(nc.sbuf_tensor("qg", [64, 4 * T], BF16))
        ustg = t0s.enter_context(nc.sbuf_tensor("ustg", [64, 8 * T], BF16))
        kvsb = t0s.enter_context(nc.sbuf_tensor("kvsb", [64, 520], F32))
        kvr = t0s.enter_context(nc.sbuf_tensor("kvr", [64, 520], F32))
        kvr_bf = t0s.enter_context(nc.sbuf_tensor("kvr_bf", [64, 520], BF16))
        nrm_g = [t0s.enter_context(nc.sbuf_tensor(f"nrm_g{j}", [128, T], F32))
                 for j in range(2)]
        nrm_g = [t_[:] for t_ in nrm_g]
        nrow = t0s.enter_context(nc.sbuf_tensor("nrow", [1, 4 * 512], F32))
        nstg = [t0s.enter_context(nc.sbuf_tensor(f"nstg{j}", [64, 512], BF16))
                for j in range(2)]
        nstg = [t_[:] for t_ in nstg]
        kp = t0s.enter_context(nc.psum_tensor("kp", [128, 512], F32))
        vp = t0s.enter_context(nc.psum_tensor("vp", [128, 512], F32))
        kva = t0s.enter_context(nc.psum_tensor("kva", [64, 260], F32))
        kvb = t0s.enter_context(nc.psum_tensor("kvb", [64, 260], F32))
        qp = t0s.enter_context(nc.psum_tensor("qp", [128, 512], F32))
        qkp = [t0s.enter_context(nc.psum_tensor(f"qkp{j}", [65, 512], F32))
               for j in range(2)]
        rbp = t0s.enter_context(nc.psum_tensor("rbp", [64, 512], F32))

        sKP = sem("sKP")
        sKE = sem("sKE")
        sVP = sem("sVP")
        sVE = sem("sVE")
        sKVM = sem("sKVM")
        sKVE = sem("sKVE")
        sKVS = sem("sKVS")
        sCC = sem("sCC")
        sKVB = sem("sKVB")
        sKVC = sem("sKVC")
        sQP = sem("sQP")
        sQE = sem("sQE")
        sQKM = sem("sQKM")
        sQKE = sem("sQKE")
        sNR = sem("sNR")
        sNRW = sem("sNRW")
        sRB = sem("sRB")
        sNST = [sem("sNST0"), sem("sNST1")]

        FG2 = 16 * 2 * NT

        def _twar(s_, val):
            def f(v):
                v.wait_ge(s_.h, val)
            return f

        tw_blocks = []
        for g in range(4):
            tw_blocks.append((
                (lambda g: lambda c: wk0[128 * c:128 * (c + 1), 512 * g:512 * (g + 1)])(g),
                t_wk, 512, _twar(sQP, 4 * NTB * g) if g >= 1 else None))
            tw_blocks.append((
                (lambda g: lambda c: wv0[128 * c:128 * (c + 1), 512 * g:512 * (g + 1)])(g),
                t_wv, 512, _twar(sVP, NT * g) if g >= 1 else None))
            tw_blocks.append((
                (lambda g: lambda c: wq0[128 * c:128 * (c + 1), 512 * g:512 * (g + 1)])(g),
                t_wk, 512, _twar(sKP, NT * (g + 1))))
        stTW = wstream_dma("t0w", t0s, tw_blocks,
                           first_wait=[(sFS[0].h, FG2), (sFS[1].h, FG2)],
                           stg_cols=512)
        bTWB = stTW["i"]

        @block.tensor
        def _(t):
            for g in range(4):
                t.wait_ge(wsC.h, bTWB + NCH * (3 * g + 1))
                for m in range(NT):
                    im = g * NT + m
                    if im >= 1:
                        t.wait_ge(sKE.h, im)
                    for c in range(NCH):
                        mm = t.matmul(kp[:], xnT[:, T * c + 128 * m:T * c + 128 * (m + 1)],
                                 t_wk[:, 512 * c:512 * (c + 1)],
                                 start=(c == 0), stop=(c == NCH - 1))
                    sKP.inc(mm, 1)
                    if m == 0:
                        t.wait_ge(wsC.h, bTWB + NCH * (3 * g + 2))
                    if im >= 1:
                        t.wait_ge(sVE.h, 1 + im)
                    for c in range(NCH):
                        mm = t.matmul(vp[:], xnT[:, T * c + 128 * m:T * c + 128 * (m + 1)],
                                 t_wv[:, 512 * c:512 * (c + 1)],
                                 start=(c == 0), stop=(c == NCH - 1))
                    sVP.inc(mm, 1)
                    if m == 0 and g >= 1:
                        t.wait_ge(sKVE.h, g)
                    t.wait_ge(sKE.h, im + 1)
                    t.wait_ge(sVE.h, 1 + im + 1)
                    for j in range(8):
                        dst = kva if j < 4 else kvb
                        mm = t.matmul(dst[:, 65 * (j % 4):65 * (j % 4 + 1)],
                                 ktile[:, 64 * j:64 * (j + 1)],
                                 v_sb[:, 65 * j:65 * (j + 1)],
                                 start=(m == 0 and j % 4 == 0),
                                 stop=(m == NT - 1 and j % 4 == 3))
                    sKVM.inc(mm, 1)
                t.wait_ge(wsC.h, bTWB + NCH * (3 * g + 3))
                for gh in range(2):
                    for ip in range(2):
                        ic = 2 * gh + ip
                        for tb in range(NTB):
                            qidx = sQP.n
                            if qidx >= 1:
                                t.wait_ge(sQE.h, qidx)
                            for c in range(NCH):
                                mm = t.matmul(qp[:],
                                         t_wk[:, 512 * c + 128 * ic:512 * c + 128 * (ic + 1)],
                                         xnT[:, T * c + 512 * tb:T * c + 512 * (tb + 1)],
                                         start=(c == 0), stop=(c == NCH - 1))
                            sQP.inc(mm, 1)
                    t.wait_ge(sKVC.h, g + 1)
                    for jj in range(4):
                        j = 4 * gh + jj
                        for tb in range(NTB):
                            qk = sQKM.n
                            if qk >= 2:
                                t.wait_ge(sQKE.h, qk - 1)
                            t.wait_ge(sQE.h, g * 4 * NTB + gh * 2 * NTB
                                      + (jj // 2) * NTB + tb + 1)
                            sQKM.inc(t.matmul(qkp[qk % 2][:],
                                     kvr_bf[:, 65 * j:65 * (j + 1)],
                                     qg[:, T * jj + 512 * tb:T * jj + 512 * (tb + 1)],
                                     start=True, stop=True), 1)
                for tb in range(NTB):
                    for j in range(8):
                        if j == 0:
                            t.wait_ge(sNRW.h, 32 * (g * NTB + tb) + 16)
                        elif j == 4:
                            t.wait_ge(sNRW.h, 32 * (g * NTB + tb + 1))
                        ridx = sRB.n
                        if ridx >= 1:
                            t.wait_ge(sNMU.h, ridx)
                        sRB.inc(t.matmul(rbp[:], onesrf_sb[0:1, 0:64],
                                 nrow[0:1, 512 * (j % 4):512 * (j % 4 + 1)],
                                 start=True, stop=True), 1)

        @block.scalar
        def _(s):
            for g in range(4):
                for m in range(NT):
                    im = g * NT + m
                    s.wait_ge(sKP.h, im + 1)
                    if im >= 1:
                        s.wait_ge(sKVM.h, im)
                    sKE.inc(s.activation(ktile[:], kp[:], AF.Relu), 1)
                    s.drain()
                for gh in range(2):
                    for ip in range(2):
                        for tb in range(NTB):
                            qidx = g * 4 * NTB + gh * 2 * NTB + ip * NTB + tb
                            s.wait_ge(sQP.h, qidx + 1)
                            if tb == 0 and ip == 0:
                                s.wait_ge(sQKM.h, (2 * g + gh) * 4 * NTB)
                            s.activation(qg[:, T * (2 * ip) + 512 * tb:
                                             T * (2 * ip) + 512 * (tb + 1)],
                                         qp[0:64, :], AF.Relu)
                            sQE.inc(s.activation(qg[:, T * (2 * ip + 1) + 512 * tb:
                                             T * (2 * ip + 1) + 512 * (tb + 1)],
                                         qp[64:128, :], AF.Relu), 1)
                            s.drain()

        @block.vector
        def _(v):
            v.wait_ge(sFS[0].h, FG2)
            v.wait_ge(sFS[1].h, FG2)
            v.memset(nrm_g[0][:], 1.0)
            v.memset(nrm_g[1][:], 1.0)
            _i = v.memset(v_sb[:, 64:520:65], 1.0)
            v.drain()
            sVE.inc(_i, 1)
            for g in range(4):
                wstream_casts(v, stTW, 2)
                for m in range(NT):
                    im = g * NT + m
                    v.wait_ge(sVP.h, im + 1)
                    if im >= 1:
                        v.wait_ge(sKVM.h, im)
                    for j in range(8):
                        _i = v.tensor_copy(v_sb[:, 65 * j:65 * j + 64],
                                           vp[:, 64 * j:64 * (j + 1)])
                    v.drain()
                    sVE.inc(_i, 1)
                wstream_casts(v, stTW, 1)
                v.wait_ge(sKVM.h, (g + 1) * NT)
                v.wait_ge(sKVS.h, 16 * g)
                v.tensor_copy(kvsb[:, 0:260], kva[:])
                _i = v.tensor_copy(kvsb[:, 260:520], kvb[:])
                v.drain()
                sKVE.inc(_i, 1)
                v.wait_ge(sKVB.h, 16 * (g + 1))
                if g >= 1:
                    v.wait_ge(sQKM.h, 8 * NTB * g)
                _i = v.tensor_copy(kvr_bf[:], kvr[:])
                v.drain()
                sKVC.inc(_i, 1)
                for gh in range(2):
                    for jj in range(4):
                        j = 4 * gh + jj
                        for tb in range(NTB):
                            qidx = ((g * 8 + j) * NTB + tb)
                            v.wait_ge(sQKM.h, qidx + 1)
                            if j == 0 and tb == 0 and g >= 1:
                                v.wait_ge(sNRW.h, 32 * g * NTB)
                            v.tensor_copy(
                                nrm_g[j // 4][32 * (j % 4):32 * (j % 4) + 1,
                                              512 * tb:512 * (tb + 1)],
                                qkp[qidx % 2][64:65, :])
                            _i = v.tensor_copy(ustg[:, T * j + 512 * tb:
                                                T * j + 512 * (tb + 1)],
                                           qkp[qidx % 2][0:64, :])
                            v.drain()
                            sQKE.inc(_i, 1)
                for half2 in range(2):
                    v.tensor_scalar(nrm_g[half2], nrm_g[half2], 1e-6, None, OP.max)
                    v.drain()
                    v.reciprocal(nrm_g[half2], nrm_g[half2])
                    v.drain()
                _i = v.tensor_scalar(nrm_g[1], nrm_g[1], 1.0, None, OP.mult)
                v.drain()
                sNR.inc(_i, 1)
                for tb in range(NTB):
                    for j in range(8):
                        ridx = sNMU.n
                        v.wait_ge(sRB.h, ridx + 1)
                        if ridx >= 2:
                            v.wait_ge(sNST[ridx % 2].h, 16 * (ridx // 2))
                        _i = v.tensor_tensor(nstg[ridx % 2],
                                        ustg[:, T * j + 512 * tb:
                                             T * j + 512 * (tb + 1)],
                                        rbp[:], OP.mult)
                        v.drain()
                        sNMU.inc(_i, 1)

        @block.gpsimd
        def _(gp):
            for g in range(4):
                gp.wait_ge(sKVE.h, g + 1)
                sKVS.inc(gp.dma_start(out=kv_in[g].ap(), in_=kvsb[:]), 16)
                gp.wait_ge(sKVS.h, 16 * (g + 1))
                sCC.inc(gp.collective_compute(
                    "AllReduce", OP.add,
                    replica_groups=[[0, 1], [2, 3], [4, 5], [6, 7]],
                    ins=[kv_in[g].ap().opt()], outs=[kv_out[g].ap().opt()]), 1)
                gp.wait_ge(sCC.h, g + 1)
                if g >= 1:
                    gp.wait_ge(sQKM.h, 8 * NTB * g)
                sKVB.inc(gp.dma_start(out=kvr[:], in_=kv_out[g].ap()), 16)
                for tb in range(NTB):
                    gp.wait_ge(sNR.h, g + 1)
                    if g + tb > 0:
                        gp.wait_ge(sRB.h, 8 * (g * NTB + tb))
                    sNRW.inc(gp.dma_start(
                        out=nrow[0:1, 0:2048],
                        in_=nrm_g[0][0:128:32, 512 * tb:512 * (tb + 1)]), 16)
                    for j in range(8):
                        if j == 4:
                            gp.wait_ge(sRB.h, 8 * (g * NTB + tb) + 4)
                            sNRW.inc(gp.dma_start(
                                out=nrow[0:1, 0:2048],
                                in_=nrm_g[1][0:128:32,
                                             512 * tb:512 * (tb + 1)]), 16)
                        nidx = (g * NTB + tb) * 8 + j
                        gp.wait_ge(sNMU.h, nidx + 1)
                        hg = 8 * g + j
                        cg = hg // 2
                        half = hg % 2
                        sNST[nidx % 2].inc(gp.dma_start(
                            out=qkvT_d[cg, 64 * half:64 * (half + 1),
                                       512 * tb:512 * (tb + 1)],
                            in_=nstg[nidx % 2]), 16)

        t0s.close()

        w0s = ExitStack()
        w_wo = w0s.enter_context(nc.sbuf_tensor("w_wo", [128, NCH * D], BF16))
        w_at = [w0s.enter_context(nc.sbuf_tensor(f"w_at{j}", [128, NCH * 128], BF16))
                for j in range(2)]
        w_at = [t_[:] for t_ in w_at]
        w_xi = [w0s.enter_context(nc.sbuf_tensor(f"w_xi{j}", [128, 512], F32))
                for j in range(2)]
        w_xi = [t_[:] for t_ in w_xi]
        w_xo = [w0s.enter_context(nc.sbuf_tensor(f"w_xo{j}", [128, 512], F32))
                for j in range(2)]
        w_xo = [t_[:] for t_ in w_xo]
        w_ps = [w0s.enter_context(nc.psum_tensor(f"w_ps{j}", [128, 512], F32))
                for j in range(2)]
        sWM = sem("sWM")
        sWE = sem("sWE")
        sWAT = [sem("sWAT0"), sem("sWAT1")]
        sWXI = [sem("sWXI0"), sem("sWXI1")]

        NST2 = 16 * NQKV // 2
        wo_blocks = [(lambda c: wo0[128 * c:128 * (c + 1), :], w_wo, D, None)]
        stWO = wstream_dma("w0w", w0s, wo_blocks,
                           first_wait=[(sNMU.h, NQKV), (sNST[0].h, NST2),
                                       (sNST[1].h, NST2)], stg_cols=D)
        bWOB = stWO["i"]

        @block.sync
        def _(sync):
            for m in range(NT):
                if m >= 2:
                    sync.wait_ge(sWM.h, 4 * (m - 1))
                else:
                    sync.wait_ge(sNST[0].h, NST2)
                    sync.wait_ge(sNST[1].h, NST2)
                for c in range(NCH):
                    sWAT[m % 2].inc(sync.dma_start(
                        out=w_at[m % 2][:, 128 * c:128 * (c + 1)],
                        in_=qkvT_d[c, :, 128 * m:128 * (m + 1)]), 16)
                for ob in range(4):
                    i = 4 * m + ob
                    if i >= 2:
                        sync.wait_ge(sWE.h, i - 1)
                    iF = ob * NT + m
                    sync.wait_ge(sFS[iF % 2].h, 16 * (iF // 2 + 1))
                    sWXI[i % 2].inc(sync.dma_start(
                        out=w_xi[i % 2], in_=xA[m, :, 512 * ob:512 * (ob + 1)]), 16)

        @block.tensor
        def _(t):
            t.wait_ge(sBR.h, 1)
            t.wait_ge(wsC.h, bWOB + NCH)
            for m in range(NT):
                t.wait_ge(sWAT[m % 2].h, 16 * NCH * (m // 2 + 1))
                for ob in range(4):
                    i = 4 * m + ob
                    if i >= 2:
                        t.wait_ge(sWE.h, i - 1)
                    t.matmul(w_ps[i % 2][:], onesr_sb[:],
                             brow_bf[0:1, 512 * ob:512 * (ob + 1)],
                             start=True, stop=False)
                    for c in range(NCH):
                        mm = t.matmul(w_ps[i % 2][:],
                                 w_at[m % 2][:, 128 * c:128 * (c + 1)],
                                 w_wo[:, D * c + 512 * ob:D * c + 512 * (ob + 1)],
                                 start=False, stop=(c == NCH - 1))
                    sWM.inc(mm, 1)

        @block.vector
        def _(v):
            wstream_casts(v, stWO, 1)
            for m in range(NT):
                for ob in range(4):
                    i = 4 * m + ob
                    v.wait_ge(sWM.h, i + 1)
                    v.wait_ge(sWXI[i % 2].h, 16 * (i // 2 + 1))
                    if i >= 2:
                        v.wait_ge(sX1S[i % 2].h, 16 * (i // 2))
                    _i = v.tensor_tensor(w_xo[i % 2], w_xi[i % 2], w_ps[i % 2][:], OP.add)
                    v.drain()
                    sWE.inc(_i, 1)

        @block.gpsimd
        def _(gp):
            for m in range(NT):
                for ob in range(4):
                    i = 4 * m + ob
                    gp.wait_ge(sWE.h, i + 1)
                    sX1S[i % 2].inc(gp.dma_start(
                        out=x1d[m, :, 512 * ob:512 * (ob + 1)],
                        in_=w_xo[i % 2]), 16)

        w0s.close()
        l0o.close()
        sBS = [sem("sBS0"), sem("sBS1")]
        sOUT = [sem("sOUT0"), sem("sOUT1")]
        sQTS = [sem("sQTS0"), sem("sQTS1")]
        sKTS = [sem("sKTS0"), sem("sKTS1")]
        sVOS = [sem("sVOS0"), sem("sVOS1")]
        sAGK = sem("sAGK")
        sAGV = sem("sAGV")
        sTSB = sem("sTSB")

        l1a = ExitStack()
        xnT1 = l1a.enter_context(nc.sbuf_tensor("xnT1", [128, NCH * T], BF16))
        X1G2 = 16 * 2 * NT
        bTR1, ntr1, bXN1 = phase_stats_xn(
            "a1", l1a, x1d, xnT1, None,
            lambda m: [(sX1S[0].h, X1G2), (sX1S[1].h, X1G2)])
        TRTOT1 = bTR1 + ntr1 * NT
        XN1TOT = bXN1 + NT

        p1s = ExitStack()
        wbuf = p1s.enter_context(nc.sbuf_tensor("wbuf", [128, NCH * D], BF16))
        kstg = [p1s.enter_context(nc.sbuf_tensor(f"kstg{j}", [128, 512], BF16)) for j in range(2)]
        kstg = [t_[:] for t_ in kstg]
        vstg = [p1s.enter_context(nc.sbuf_tensor(f"vstg{j}", [128, 512], BF16)) for j in range(2)]
        vstg = [t_[:] for t_ in vstg]
        qstg = [p1s.enter_context(nc.sbuf_tensor(f"qstg{j}", [128, 512], BF16)) for j in range(2)]
        qstg = [t_[:] for t_ in qstg]
        pp = [p1s.enter_context(nc.psum_tensor(f"pp{j}", [128, 512], F32))
              for j in range(2)]
        sKM1 = sem("sKM1")
        sKE1 = sem("sKE1")
        sVM1 = sem("sVM1")
        sVE1 = sem("sVE1")
        sQM1 = sem("sQM1")
        sQE1 = sem("sQE1")
        NKIT = 4 * NCH * NTB // 4
        NKITER = NCH * NTB
        NVITER = 4 * NT

        def _p1war(s_, val):
            def f(v):
                v.wait_ge(s_.h, val)
            return f

        p1_blocks = [
            (lambda c: wk1[128 * c:128 * (c + 1), :], wbuf, D, None),
            (lambda c: wv1[128 * c:128 * (c + 1), :], wbuf, D,
             _p1war(sKM1, NKITER)),
            (lambda c: wq1[128 * c:128 * (c + 1), :], wbuf, D,
             _p1war(sVM1, NVITER)),
        ]
        stP1 = wstream_dma("p1w", p1s, p1_blocks,
                           first_wait=[(ssXN.h, XN1TOT), (ssTR.h, TRTOT1)],
                           stg_cols=D)
        bP1B = stP1["i"]

        @block.tensor
        def _(t):
            t.wait_ge(wsC.h, bP1B + NCH)
            it = 0
            for ch in range(NCH):
                for tb in range(NTB):
                    if it >= 2:
                        t.wait_ge(sKE1.h, it - 1)
                    for c in range(NCH):
                        mm = t.matmul(pp[it % 2][:],
                                 wbuf[:, D * c + 128 * ch:D * c + 128 * (ch + 1)],
                                 xnT1[:, T * c + 512 * tb:T * c + 512 * (tb + 1)],
                                 start=(c == 0), stop=(c == NCH - 1))
                    sKM1.inc(mm, 1)
                    it += 1
            t.wait_ge(wsC.h, bP1B + 2 * NCH)
            it = 0
            for ob in range(4):
                for m in range(NT):
                    if it >= 2:
                        t.wait_ge(sVE1.h, it - 1)
                    t.matmul(pp[it % 2][:], onesr_sb[:],
                             brow_bf[0:1, 2 * D + 512 * ob:2 * D + 512 * (ob + 1)],
                             start=True, stop=False)
                    for c in range(NCH):
                        mm = t.matmul(pp[it % 2][:],
                                 xnT1[:, T * c + 128 * m:T * c + 128 * (m + 1)],
                                 wbuf[:, D * c + 512 * ob:D * c + 512 * (ob + 1)],
                                 start=False, stop=(c == NCH - 1))
                    sVM1.inc(mm, 1)
                    it += 1
            t.wait_ge(wsC.h, bP1B + 3 * NCH)
            it = 0
            for ch in range(NCH):
                for tb in range(NTB):
                    if it >= 2:
                        t.wait_ge(sQE1.h, it - 1)
                    for c in range(NCH):
                        mm = t.matmul(pp[it % 2][:],
                                 wbuf[:, D * c + 128 * ch:D * c + 128 * (ch + 1)],
                                 xnT1[:, T * c + 512 * tb:T * c + 512 * (tb + 1)],
                                 start=(c == 0), stop=(c == NCH - 1))
                    sQM1.inc(mm, 1)
                    it += 1

        @block.vector
        def _(v):
            wstream_casts(v, stP1, 1)
            it = 0
            for ch in range(NCH):
                for tb in range(NTB):
                    v.wait_ge(sKM1.h, it + 1)
                    if it >= 2:
                        v.wait_ge(sKTS[it % 2].h, 16 * (it // 2))
                    _i = v.tensor_scalar(kstg[it % 2], pp[it % 2][:],
                                    bqk_sb[:, NCH + ch:NCH + ch + 1], None, OP.add)
                    v.drain()
                    sKE1.inc(_i, 1)
                    it += 1
            wstream_casts(v, stP1, 1)
            it = 0
            for ob in range(4):
                for m in range(NT):
                    v.wait_ge(sVM1.h, it + 1)
                    if it >= 2:
                        v.wait_ge(sVOS[it % 2].h, 16 * (it // 2))
                    _i = v.tensor_copy(vstg[it % 2], pp[it % 2][:])
                    v.drain()
                    sVE1.inc(_i, 1)
                    it += 1
            wstream_casts(v, stP1, 1)
            it = 0
            for ch in range(NCH):
                for tb in range(NTB):
                    v.wait_ge(sQM1.h, it + 1)
                    if it >= 2:
                        v.wait_ge(sQTS[it % 2].h, 16 * (it // 2))
                    _i = v.tensor_scalar(qstg[it % 2], pp[it % 2][:],
                                    bqk_sb[:, ch:ch + 1], None, OP.add)
                    v.drain()
                    sQE1.inc(_i, 1)
                    it += 1

        @block.gpsimd
        def _(gp):
            it = 0
            for ch in range(NCH):
                for tb in range(NTB):
                    gp.wait_ge(sKE1.h, it + 1)
                    sKTS[it % 2].inc(gp.dma_start(
                        out=kT_own.ap()[ch, :, 512 * tb:512 * (tb + 1)],
                        in_=kstg[it % 2]), 16)
                    it += 1
            it = 0
            for ob in range(4):
                for m in range(NT):
                    gp.wait_ge(sVE1.h, it + 1)
                    sVOS[it % 2].inc(gp.dma_start(
                        out=v_own.ap()[m, :, 512 * ob:512 * (ob + 1)],
                        in_=vstg[it % 2]), 16)
                    it += 1
            gp.wait_ge(sKTS[0].h, 16 * ((NKITER + 1) // 2))
            gp.wait_ge(sKTS[1].h, 16 * (NKITER // 2))
            sAGK.inc(gp.collective_compute(
                "AllGather", OP.bypass,
                replica_groups=[[0, 1], [2, 3], [4, 5], [6, 7]],
                ins=[kT_own.ap().opt()], outs=[kT_all.ap().opt()]), 1)
            gp.wait_ge(sVOS[0].h, 16 * ((NVITER + 1) // 2))
            gp.wait_ge(sVOS[1].h, 16 * (NVITER // 2))
            sAGV.inc(gp.collective_compute(
                "AllGather", OP.bypass,
                replica_groups=[[0, 1], [2, 3], [4, 5], [6, 7]],
                ins=[v_own.ap().opt()], outs=[v_all.ap().opt()]), 1)
            it = 0
            for ch in range(NCH):
                for tb in range(NTB):
                    gp.wait_ge(sQE1.h, it + 1)
                    sQTS[it % 2].inc(gp.dma_start(
                        out=qT_d[ch, :, 512 * tb:512 * (tb + 1)],
                        in_=qstg[it % 2]), 16)
                    it += 1

        p1s.close()

        g1s = ExitStack()
        gwb = g1s.enter_context(nc.sbuf_tensor("gwb", [128, NCH * 8], BF16))
        awb = g1s.enter_context(nc.sbuf_tensor("awb", [128, NCH * 128], BF16))
        bwstg = g1s.enter_context(nc.sbuf_tensor("bwstg", [128, D], F32))
        bwb = g1s.enter_context(nc.sbuf_tensor("bwb", [128, D], BF16))
        tsb = g1s.enter_context(nc.sbuf_tensor("tsb", [128, T], BF16))
        cwT = g1s.enter_context(nc.sbuf_tensor("cwT", [8, T], BF16))
        lgT = g1s.enter_context(nc.sbuf_tensor("lgT", [8, 512], BF16))
        cwxs = g1s.enter_context(nc.sbuf_tensor("cwxs", [128, 512], BF16))
        lg_f = g1s.enter_context(nc.sbuf_tensor("lg_f", [128, 8], F32))
        gw1a = g1s.enter_context(nc.sbuf_tensor("gw1a", [128, 8], F32))
        gw1b = g1s.enter_context(nc.sbuf_tensor("gw1b", [128, 8], F32))
        gw1c = g1s.enter_context(nc.sbuf_tensor("gw1c", [128, 8], F32))
        gcol = g1s.enter_context(nc.sbuf_tensor("gcol", [128, 8], F32))
        e_sb = g1s.enter_context(nc.sbuf_tensor("e_sb", [128, 8], F32))
        ed_sb = g1s.enter_context(nc.sbuf_tensor("ed_sb", [128, 1], F32))
        cw_bf = g1s.enter_context(nc.sbuf_tensor("cw_bf", [128, 8], BF16))
        lg_ps = g1s.enter_context(nc.psum_tensor("lg_ps", [8, 512], F32))
        ltp = [g1s.enter_context(nc.psum_tensor(f"ltp{j}", [128, 8], BF16))
               for j in range(2)]
        ctp = g1s.enter_context(nc.psum_tensor("ctp", [8, 128], BF16))
        t_ps = g1s.enter_context(nc.psum_tensor("t_ps", [128, 512], F32))
        cwx = g1s.enter_context(nc.psum_tensor("cwx", [128, 512], F32))

        sLGM = sem("sLGM")
        sLGC = sem("sLGC")
        sLTP = sem("sLTP")
        sLTC = sem("sLTC")
        sGV = sem("sGV")
        sGS = sem("sGS")
        sGU = sem("sGU")
        sCWV = sem("sCWV")
        sCTP = sem("sCTP")
        sCWT = sem("sCWT")
        sTPM = sem("sTPM")
        sBWB = sem("sBWB")

        NQIT = NCH * NTB
        P1GATE = [(sQTS[0].h, 16 * ((NQIT + 1) // 2)),
                  (sQTS[1].h, 16 * (NQIT // 2))]
        g1_blocks = [
            (lambda c: gw1[128 * c:128 * (c + 1), :], gwb, 8, None),
            (lambda c: aw1[128 * c:128 * (c + 1), :], awb, 128, None),
        ]
        @block.sync
        def _(sync):
            for (h_, v_) in P1GATE:
                sync.wait_ge(h_, v_)
            sBWB.inc(sync.dma_start(out=bwstg[:], in_=bw1), 16)

        stG1 = wstream_dma("g1w", g1s, g1_blocks, first_wait=P1GATE,
                           stg_cols=128)
        bG1B = stG1["i"]

        @block.vector
        def _(v):
            v.wait_ge(sBWB.h, 16)
            _i = v.tensor_copy(bwb[:], bwstg[:])
            v.drain()
            sBWB.inc(_i, 1)
            wstream_casts(v, stG1, 2)
            for tb in range(NTB):
                v.wait_ge(sLGM.h, tb + 1)
                _i = v.tensor_copy(lgT[:], lg_ps[:])
                v.drain()
                sLGC.inc(_i, 1)
                for s4 in range(4):
                    idx = 4 * tb + s4
                    v.wait_ge(sLTP.h, idx + 1)
                    _i = v.tensor_copy(lg_f[:], ltp[idx % 2][:])
                    v.drain()
                    sLTC.inc(_i, 1)
                    v.reduce_max(gcol[:, 0:1], lg_f[:], axis=AX)
                    v.drain()
                    v.tensor_scalar(gcol[:, 1:2], gcol[:, 0:1], -1.0, None, OP.mult)
                    v.drain()
                    v.tensor_scalar(gw1a[:], lg_f[:], gcol[:, 0:1], None, OP.is_ge)
                    v.drain()
                    v.scalar_tensor_tensor(gw1b[:], gw1a[:], -100000.0, lg_f[:],
                                           OP.mult, OP.add)
                    v.drain()
                    _i = v.reduce_max(gcol[:, 2:3], gw1b[:], axis=AX)
                    v.drain()
                    sGV.inc(_i, 1)
                    v.tensor_scalar(gw1c[:], lg_f[:], gcol[:, 2:3], None, OP.is_ge)
                    v.drain()
                    v.wait_ge(sGS.h, idx + 1)
                    _i = v.tensor_tensor(gw1a[:], e_sb[:], gw1c[:], OP.mult)
                    v.drain()
                    sGU.inc(_i, 1)
                    v.tensor_scalar(gcol[:, 3:4], ed_sb[:], 1.0, None, OP.add)
                    v.drain()
                    v.reciprocal(gcol[:, 4:5], gcol[:, 3:4])
                    v.drain()
                    _i = v.tensor_scalar(cw_bf[:], gw1a[:], gcol[:, 4:5], None, OP.mult)
                    v.drain()
                    sCWV.inc(_i, 1)
                    v.wait_ge(sCTP.h, idx + 1)
                    _i = v.tensor_copy(cwT[0:8, 128 * idx:128 * (idx + 1)], ctp[:])
                    v.drain()
                    sCWT.inc(_i, 1)
            for tb in range(NTB):
                v.wait_ge(sTPM.h, tb + 1)
                v.tensor_copy(cwxs[:], cwx[:])
                v.drain()
                _i = v.tensor_tensor(tsb[:, 512 * tb:512 * (tb + 1)], t_ps[:],
                                     cwxs[:], OP.mult)
                v.drain()
                sTSB.inc(_i, 1)

        @block.scalar
        def _(s):
            for tb in range(NTB):
                for s4 in range(4):
                    idx = 4 * tb + s4
                    s.wait_ge(sGV.h, idx + 1)
                    if idx >= 1:
                        s.wait_ge(sGU.h, idx)
                    s.activation(e_sb[:], lg_f[:], AF.Exp, bias=gcol[:, 1:2])
                    sGS.inc(s.activation(ed_sb[:], gcol[:, 2:3], AF.Exp,
                                         bias=gcol[:, 1:2]), 1)
                    s.drain()

        @block.tensor
        def _(t):
            t.wait_ge(wsC.h, bG1B + NCH)
            for tb in range(NTB):
                if tb >= 1:
                    t.wait_ge(sLGC.h, tb)
                for c in range(NCH):
                    mm = t.matmul(lg_ps[:], gwb[:, 8 * c:8 * (c + 1)],
                             xnT1[:, T * c + 512 * tb:T * c + 512 * (tb + 1)],
                             start=(c == 0), stop=(c == NCH - 1))
                sLGM.inc(mm, 1)
                for s4 in range(4):
                    idx = 4 * tb + s4
                    t.wait_ge(sLGC.h, tb + 1)
                    if idx >= 2:
                        t.wait_ge(sLTC.h, idx - 1)
                    sLTP.inc(t.transpose(ltp[idx % 2][:],
                                lgT[0:8, 128 * s4:128 * (s4 + 1)],
                                id_bf[0:8, 0:8]), 1)
                    t.wait_ge(sCWV.h, idx + 1)
                    if idx >= 1:
                        t.wait_ge(sCWT.h, idx)
                    sCTP.inc(t.transpose(ctp[:], cw_bf[:], id_bf[:]), 1)
            t.wait_ge(wsC.h, bG1B + 2 * NCH)
            for tb in range(NTB):
                if tb >= 1:
                    t.wait_ge(sTSB.h, tb)
                for c in range(NCH):
                    t.matmul(t_ps[:], awb[:, 128 * c:128 * (c + 1)],
                             xnT1[:, T * c + 512 * tb:T * c + 512 * (tb + 1)],
                             start=(c == 0), stop=(c == NCH - 1))
                t.wait_ge(sCWT.h, 4 * (tb + 1))
                sTPM.inc(t.matmul(cwx[:], expand_sb[:],
                         cwT[0:8, 512 * tb:512 * (tb + 1)],
                         start=True, stop=True), 1)

        f1_w = [g1s.enter_context(nc.sbuf_tensor(f"f1_w{j}", [128, NCH * 512], BF16)) for j in range(2)]
        f1_w = [t_[:] for t_ in f1_w]
        f1_xi = [g1s.enter_context(nc.sbuf_tensor(f"f1_xi{j}", [128, 512], F32)) for j in range(2)]
        f1_xi = [t_[:] for t_ in f1_xi]
        f1_xo = [g1s.enter_context(nc.sbuf_tensor(f"f1_xo{j}", [128, 512], F32)) for j in range(2)]
        f1_xo = [t_[:] for t_ in f1_xo]
        f1_ps = [g1s.enter_context(nc.psum_tensor(f"f1_ps{j}", [128, 512], F32))
                 for j in range(2)]
        sF1M, sF1E, sF1XI = sFM, sFE, sFXI
        bF1M, bF1E = sFM.n, sFE.n
        bF1XI = [sFXI[0].n, sFXI[1].n]

        def _f1war(ob):
            if ob < 2:
                return None
            def f(v):
                v.wait_ge(sF1M.h, bF1M + NT * (ob - 1))
            return f

        f1_blocks = [((lambda ob: lambda c: fw1[128 * c:128 * (c + 1),
                                                512 * ob:512 * (ob + 1)])(ob),
                      f1_w[ob % 2], 512, _f1war(ob)) for ob in range(4)]
        f1stg = [g1s.enter_context(nc.sbuf_tensor(f"f1w_stg{j}", [128, 512], F32))
                 for j in range(2)]
        f1stg = [t_[:] for t_ in f1stg]
        stF1 = {"stg": f1stg, "blocks": f1_blocks, "i": wst_i[0], "b": 0}
        bF1B = wst_i[0]
        wst_i[0] += NCH * len(f1_blocks)

        @block.sync
        def _(sync):
            i = bF1B
            for (h_, v_) in P1GATE:
                sync.wait_ge(h_, v_)
            for ob in range(4):
                srcfn = f1_blocks[ob][0]
                for c in range(NCH):
                    if i >= 2:
                        sync.wait_ge(wsC.h, i - 1)
                    wsD[i % 2].inc(sync.dma_start(
                        out=f1stg[i % 2][:, 0:512], in_=srcfn(c)), 16)
                    i += 1
                for m in range(NT):
                    ii = ob * NT + m
                    if ii >= 2:
                        sync.wait_ge(sF1E.h, bF1E + ii - 1)
                    sF1XI[ii % 2].inc(sync.dma_start(
                        out=f1_xi[ii % 2],
                        in_=x1d[m, :, 512 * ob:512 * (ob + 1)]), 16)


        @block.tensor
        def _(t):
            t.wait_ge(sBWB.h, 17)
            for ob in range(4):
                t.wait_ge(wsC.h, bF1B + NCH * (ob + 1))
                for m in range(NT):
                    i = ob * NT + m
                    if i >= 2:
                        t.wait_ge(sF1E.h, bF1E + i - 1)
                    t.wait_ge(sTSB.h, m // (NT // NTB) + 1)
                    for c in range(NCH):
                        t.matmul(f1_ps[i % 2][:],
                                 xnT1[:, T * c + 128 * m:T * c + 128 * (m + 1)],
                                 f1_w[ob % 2][:, 512 * c:512 * (c + 1)],
                                 start=(c == 0), stop=False)
                    sF1M.inc(t.matmul(f1_ps[i % 2][:], tsb[:, 128 * m:128 * (m + 1)],
                             bwb[:, 512 * ob:512 * (ob + 1)],
                             start=False, stop=True), 1)

        @block.vector
        def _(v):
            for ob in range(4):
                wstream_casts(v, stF1, 1)
                for m in range(NT):
                    i = ob * NT + m
                    v.wait_ge(sF1M.h, bF1M + i + 1)
                    v.wait_ge(sF1XI[i % 2].h, bF1XI[i % 2] + 16 * (i // 2 + 1))
                    if i >= 2:
                        v.wait_ge(sBS[i % 2].h, 16 * (i // 2))
                    _i = v.tensor_tensor(f1_xo[i % 2], f1_xi[i % 2],
                                          f1_ps[i % 2][:], OP.add)
                    v.drain()
                    sF1E.inc(_i, 1)

        @block.gpsimd
        def _(gp):
            for ob in range(4):
                for m in range(NT):
                    i = ob * NT + m
                    gp.wait_ge(sF1E.h, bF1E + i + 1)
                    sBS[i % 2].inc(gp.dma_start(
                        out=xB[m, :, 512 * ob:512 * (ob + 1)],
                        in_=f1_xo[i % 2]), 16)

        g1s.close()
        l1a.close()

        l1c = ExitStack()
        ktb = l1c.enter_context(nc.sbuf_tensor("ktb", [128, NCH * S], BF16))
        aqb = l1c.enter_context(nc.sbuf_tensor("aqb", [128, NCH * 512], BF16))
        mskb = l1c.enter_context(nc.sbuf_tensor("mskb", [128, MAXM * 512], BF16))
        qtile = [l1c.enter_context(nc.sbuf_tensor(f"qtile{j}", [128, 512], BF16)) for j in range(2)]
        qtile = [t_[:] for t_ in qtile]
        vtile = [l1c.enter_context(nc.sbuf_tensor(f"vtile{j}", [128, 128], BF16)) for j in range(2)]
        vtile = [t_[:] for t_ in vtile]
        ptile = [l1c.enter_context(nc.sbuf_tensor(f"ptile{j}", [128, 512], BF16)) for j in range(2)]
        ptile = [t_[:] for t_ in ptile]
        rden_bf = l1c.enter_context(nc.sbuf_tensor("rden_bf", [1, 512], BF16))
        rbs = l1c.enter_context(nc.sbuf_tensor("rbs", [128, 512], BF16))
        sc_ps = [l1c.enter_context(nc.psum_tensor(f"sc_ps{j}", [128, 512], F32))
                 for j in range(2)]
        pv_ps = [l1c.enter_context(nc.psum_tensor(f"pv_ps{j}", [128, 512], F32))
                 for j in range(2)]
        den_ps = [l1c.enter_context(nc.psum_tensor(f"den_ps{j}", [1, 512], F32))
                  for j in range(2)]
        rb_ps = l1c.enter_context(nc.psum_tensor("rb_ps", [128, 512], F32))

        sKTB = sem("sKTB")
        sMSK = sem("sMSK")
        sQTL = [sem("sQTL0"), sem("sQTL1")]
        sVTL = [sem("sVTL0"), sem("sVTL1")]
        sSC = sem("sSC")
        sEXP = sem("sEXP")
        sPMV = sem("sPMV")
        sMM2 = sem("sMM2")
        sHD, bHD = sKE, sKE.n
        sRD, bRD = sNR, sNR.n
        sRBM, bRBM = sRB, sRB.n
        sMSK, bMSK = sFW, sFW.n
        sNM1 = sem("sNM1")
        sAQ = sem("sAQ")

        BS2 = 16 * 2 * NT
        KTBTOT = 16 * 2 * NCH * NT
        CQB = {qb: [kc for kc in range(NB) if kc not in skip[qb]]
               for qb in range(NQB)}
        MQB = {qb: sorted(kc for (q2, kc) in morder if q2 == qb)
               for qb in range(NQB)}
        ready = {}
        pmv_cum = [0]
        for qb in range(NQB):
            pmv_cum.append(pmv_cum[-1] + len(MQB[qb]) * 16)
        msk_cum = [0]
        for qb in range(NQB):
            msk_cum.append(msk_cum[-1] + 16 * len(MQB[qb]))

        @block.gpsimd
        def _(gp):
            gp.wait_ge(sAGK.h, 1)
            gp.wait_ge(sBS[0].h, BS2)
            gp.wait_ge(sBS[1].h, BS2)
            for r in range(2):
                for c in range(NCH):
                    for m in range(NT):
                        sKTB.inc(gp.dma_start(
                            out=ktb[:, S * c + 128 * (2 * m + r):
                                    S * c + 128 * (2 * m + r) + 128],
                            in_=kT_all.ap()[r, c, :, 128 * m:128 * (m + 1)]), 16)

        @block.sync
        def _(sync):
            for (h_, v_) in P1GATE:
                sync.wait_ge(h_, v_)
            sync.wait_ge(sBS[0].h, BS2)
            sync.wait_ge(sBS[1].h, BS2)
            sync.wait_ge(sAGV.h, 1)
            vix = 0
            for qb in range(NQB):
                if qb >= 1:
                    sync.wait_ge(sPMV.h, pmv_cum[qb])
                for j, kc in enumerate(MQB[qb]):
                    sMSK.inc(sync.dma_start(out=mskb[:, 512 * j:512 * (j + 1)],
                                            in_=masks[midx[(qb, kc)]]), 16)
                for h in range(NCH):
                    hh = qb * NCH + h
                    if hh >= 2:
                        sync.wait_ge(sHD.h, bHD + hh - 1)
                    sQTL[hh % 2].inc(sync.dma_start(
                        out=qtile[hh % 2],
                        in_=qT_d[h, :, 512 * qb:512 * (qb + 1)]), 16)
                    for kci, kc in enumerate(CQB[qb]):
                        if vix >= 2:
                            sync.wait_ge(sMM2.h, vix - 1)
                        sVTL[vix % 2].inc(sync.dma_start(
                            out=vtile[vix % 2],
                            in_=v_all.ap()[kc % 2, kc // 2, :,
                                           128 * h:128 * (h + 1)]), 16)
                        vix += 1

        @block.scalar
        def _(s):
            six = 0
            for qb in range(NQB):
                for h in range(NCH):
                    for kci, kc in enumerate(CQB[qb]):
                        s.wait_ge(sSC.h, six + 1)
                        if six >= 2:
                            s.wait_ge(sMM2.h, six - 1)
                        sEXP.inc(s.activation(ptile[six % 2], sc_ps[six % 2][:],
                                               AF.Exp, scale=SCALE_L1), 1)
                        s.drain()
                        six += 1

        @block.vector
        def _(v):
            six = 0
            for qb in range(NQB):
                for h in range(NCH):
                    hh = qb * NCH + h
                    for kci, kc in enumerate(CQB[qb]):
                        if (qb, kc) in midx:
                            v.wait_ge(sEXP.h, six + 1)
                            v.wait_ge(sMSK.h, bMSK + msk_cum[qb + 1])
                            jm = MQB[qb].index(kc)
                            _i = v.tensor_tensor(ptile[six % 2], ptile[six % 2],
                                            mskb[:, 512 * jm:512 * (jm + 1)], OP.mult)
                            v.drain()
                            sPMV.inc(_i, 1)
                            ready[(qb, h, kc)] = (sPMV.h, sPMV.n)
                        else:
                            ready[(qb, h, kc)] = (sEXP.h, six + 1)
                        six += 1
                    v.wait_ge(sMM2.h, six)
                    if hh >= 1:
                        v.wait_ge(sRBM.h, bRBM + hh)
                    with nc.allow_low_precision(reason="rden feeds a bf16 matmul operand"):
                        _i = v.reciprocal(rden_bf[:], den_ps[hh % 2][:])
                    v.drain()
                    sRD.inc(_i, 1)
                    v.wait_ge(sRBM.h, bRBM + hh + 1)
                    v.tensor_copy(rbs[:], rb_ps[:])
                    v.drain()
                    if h == 0 and qb >= 1:
                        v.wait_ge(sAQ.h, 16 * NCH * qb)
                    _i = v.tensor_tensor(aqb[:, 512 * h:512 * (h + 1)],
                                          pv_ps[hh % 2][:], rbs[:], OP.mult)
                    v.drain()
                    sNM1.inc(_i, 1)

        @block.tensor
        def _(t):
            t.wait_ge(sKTB.h, KTBTOT)
            six = 0
            vix = 0
            for qb in range(NQB):
                for h in range(NCH):
                    hh = qb * NCH + h
                    t.wait_ge(sQTL[hh % 2].h, 16 * (hh // 2 + 1))
                    nkc = len(CQB[qb])
                    for kci, kc in enumerate(CQB[qb]):
                        if six >= 2:
                            t.wait_ge(sEXP.h, six - 1)
                        sSC.inc(t.matmul(sc_ps[six % 2][:],
                                 ktb[:, S * h + 128 * kc:S * h + 128 * (kc + 1)],
                                 qtile[hh % 2], start=True, stop=True), 1)
                        rh, rv = ready[(qb, h, kc)]
                        t.wait_ge(rh, rv)
                        t.wait_ge(sVTL[vix % 2].h, 16 * (vix // 2 + 1))
                        if kci == 0 and hh >= 2:
                            t.wait_ge(sNM1.h, hh - 1)
                        dn = t.matmul(den_ps[hh % 2][:], onesc_sb[:],
                                      ptile[six % 2],
                                      start=(kci == 0), stop=(kci == nkc - 1))
                        if kci == nkc - 1:
                            sHD.inc(dn, 1)
                        mm = t.matmul(pv_ps[hh % 2][:], vtile[vix % 2],
                                      ptile[six % 2],
                                      start=(kci == 0), stop=(kci == nkc - 1))
                        sMM2.inc(mm, 1)
                        six += 1
                        vix += 1
                    t.wait_ge(sRD.h, bRD + hh + 1)
                    if hh >= 1:
                        t.wait_ge(sNM1.h, hh)
                    sRBM.inc(t.matmul(rb_ps[:], onesr_sb[:], rden_bf[:],
                             start=True, stop=True), 1)

        @block.gpsimd
        def _(gp):
            for qb in range(NQB):
                gp.wait_ge(sNM1.h, 16 * (qb + 1))
                gp.wait_ge(sAQ.h, 16 * NCH * qb)
                for c in range(NCH):
                    sAQ.inc(gp.dma_start(out=aT_d[qb, c],
                                         in_=aqb[:, 512 * c:512 * (c + 1)]), 16)

        l1c.close()

        w1s = ExitStack()
        wbuf2 = w1s.enter_context(nc.sbuf_tensor("wbuf2", [128, NCH * D], BF16))
        atile = [w1s.enter_context(nc.sbuf_tensor(f"atile{j}", [128, NCH * 128], BF16)) for j in range(2)]
        atile = [t_[:] for t_ in atile]
        y_xi = [w1s.enter_context(nc.sbuf_tensor(f"y_xi{j}", [128, 512], F32)) for j in range(2)]
        y_xi = [t_[:] for t_ in y_xi]
        y_xo = [w1s.enter_context(nc.sbuf_tensor(f"y_xo{j}", [128, 512], F32)) for j in range(2)]
        y_xo = [t_[:] for t_ in y_xo]
        y_ps = [w1s.enter_context(nc.psum_tensor(f"y_ps{j}", [128, 512], F32))
                for j in range(2)]
        sAT, sW1M, sW1E, sW1XI = sWAT, sWM, sWE, sWXI
        bAT = [sWAT[0].n, sWAT[1].n]
        bW1M, bW1E = sWM.n, sWE.n
        bW1XI = [sWXI[0].n, sWXI[1].n]

        S1GATE = [(sNM1.h, 16 * NQB * 1), (sAQ.h, 16 * NCH * NQB),
                  (sMSK.h, bMSK + msk_cum[NQB])]
        w1_blocks = [(lambda c: wo1[128 * c:128 * (c + 1), :], wbuf2, D, None)]
        stW1 = wstream_dma("w1w", w1s, w1_blocks, first_wait=S1GATE,
                           stg_cols=D)
        bW1B = stW1["i"]

        @block.sync
        def _(sync):
            QQ = NT // NQB
            for m in range(NT):
                if m >= 2:
                    sync.wait_ge(sW1M.h, bW1M + 4 * (m - 1))
                else:
                    sync.wait_ge(sNM1.h, 16 * NQB)
                sync.wait_ge(sAQ.h, 16 * NCH * (m // QQ + 1))
                for c in range(NCH):
                    sAT[m % 2].inc(sync.dma_start(
                        out=atile[m % 2][:, 128 * c:128 * (c + 1)],
                        in_=aT_d[m // QQ, c, :, 128 * (m % QQ):128 * (m % QQ + 1)]), 16)
                for ob in range(4):
                    i = 4 * m + ob
                    if i >= 2:
                        sync.wait_ge(sW1E.h, bW1E + i - 1)
                    iB = ob * NT + m
                    sync.wait_ge(sBS[iB % 2].h, 16 * (iB // 2 + 1))
                    sW1XI[i % 2].inc(sync.dma_start(
                        out=y_xi[i % 2], in_=xB[m, :, 512 * ob:512 * (ob + 1)]), 16)

        @block.tensor
        def _(t):
            t.wait_ge(wsC.h, bW1B + NCH)
            for m in range(NT):
                t.wait_ge(sAT[m % 2].h, bAT[m % 2] + 16 * NCH * (m // 2 + 1))
                for ob in range(4):
                    i = 4 * m + ob
                    if i >= 2:
                        t.wait_ge(sW1E.h, bW1E + i - 1)
                    t.matmul(y_ps[i % 2][:], onesr_sb[:],
                             brow_bf[0:1, 3 * D + 512 * ob:3 * D + 512 * (ob + 1)],
                             start=True, stop=False)
                    for c in range(NCH):
                        mm = t.matmul(y_ps[i % 2][:],
                                 atile[m % 2][:, 128 * c:128 * (c + 1)],
                                 wbuf2[:, D * c + 512 * ob:D * c + 512 * (ob + 1)],
                                 start=False, stop=(c == NCH - 1))
                    sW1M.inc(mm, 1)

        @block.vector
        def _(v):
            wstream_casts(v, stW1, 1)
            for m in range(NT):
                for ob in range(4):
                    i = 4 * m + ob
                    v.wait_ge(sW1M.h, bW1M + i + 1)
                    v.wait_ge(sW1XI[i % 2].h, bW1XI[i % 2] + 16 * (i // 2 + 1))
                    if i >= 2:
                        v.wait_ge(sOUT[i % 2].h, 16 * (i // 2))
                    _i = v.tensor_tensor(y_xo[i % 2], y_xi[i % 2], y_ps[i % 2][:], OP.add)
                    v.drain()
                    sW1E.inc(_i, 1)

        @block.gpsimd
        def _(gp):
            for m in range(NT):
                for ob in range(4):
                    i = 4 * m + ob
                    gp.wait_ge(sW1E.h, bW1E + i + 1)
                    sOUT[i % 2].inc(gp.dma_start(
                        out=out[m, :, 512 * ob:512 * (ob + 1)],
                        in_=y_xo[i % 2]), 16)
            gp.wait_ge(sOUT[0].h, 16 * 2 * NT)
            gp.wait_ge(sOUT[1].h, 16 * 2 * NT)

        if dbg_x1:
            sDBG = ssL
            bDBG = ssL.n

            @block.sync
            def _(sync):
                sync.wait_ge(sX1S[0].h, X1G2)
                sync.wait_ge(sX1S[1].h, X1G2)
                for m in range(NT):
                    sDBG.inc(sync.dma_start(out=outdbg[m], in_=x1d[m]), 16)
                sync.wait_ge(sDBG.h, bDBG + 16 * NT)

        w1s.close()
    return nc


_CACHE = {}


def _get_nc(S, dbg):
    key = (S, dbg)
    if key not in _CACHE:
        _CACHE[key] = build(S, dbg)
    return _CACHE[key]


def _f32c(a):
    return np.ascontiguousarray(np.asarray(a, dtype=np.float32))


def make_in_maps(x, w, S):
    B = x.shape[0]
    NT = (S // 2) // 128
    _, midx, morder = _mask_info(S)
    xr = np.asarray(x, dtype=np.float32).reshape(B, S // 128, 128, D)
    base = dict(
        wq0=_f32c(w["L0_attn_Wq"]), wk0=_f32c(w["L0_attn_Wk"]),
        wv0=_f32c(w["L0_attn_Wv"]), wo0=_f32c(w["L0_attn_Wo"]),
        fw0=_f32c(w["L0_ffn_weight"]),
        bo0=_f32c(w["L0_attn_bo"]).reshape(1, D),
        fb0=_f32c(w["L0_ffn_bias"]).reshape(1, D),
        wq1=_f32c(np.asarray(w["L1_attn_Wq"]).T),
        wk1=_f32c(np.asarray(w["L1_attn_Wk"]).T),
        wv1=_f32c(np.asarray(w["L1_attn_Wv"]).T),
        wo1=_f32c(np.asarray(w["L1_attn_Wo"]).T),
        fw1=_f32c(np.asarray(w["L1_ffn_weight"]).T),
        bq1=_f32c(np.asarray(w["L1_attn_bq"]).reshape(NCH, 128).T),
        bk1=_f32c(np.asarray(w["L1_attn_bk"]).reshape(NCH, 128).T),
        bv1=_f32c(w["L1_attn_bv"]).reshape(1, D),
        bo1=_f32c(w["L1_attn_bo"]).reshape(1, D),
        gw1=_f32c(np.asarray(w["L1_ffn_gate_w"]).T),
        aw1=_f32c(np.asarray(w["L1_ffn_A_w"]).reshape(128, D).T),
        bw1=_f32c(np.asarray(w["L1_ffn_B_w"]).transpose(0, 2, 1).reshape(128, D)),
    )
    masks_p = [_masks_for_parity(S, p, morder) for p in (0, 1)]
    in_maps = []
    for c in range(8):
        b, p = c // 2, c % 2
        im = dict(base)
        im["xin"] = np.ascontiguousarray(xr[b, p::2])
        im["masks"] = masks_p[p]
        in_maps.append(im)
    return in_maps


def kernel(L0_attn_hidden_states, L0_attn_Wq, L0_attn_Wk, L0_attn_Wv,
           L0_attn_Wo, L0_attn_bo, L0_ffn_weight, L0_ffn_bias,
           L1_attn_Wq, L1_attn_bq, L1_attn_Wk, L1_attn_bk, L1_attn_Wv,
           L1_attn_bv, L1_attn_Wo, L1_attn_bo, L1_ffn_weight,
           L1_ffn_gate_w, L1_ffn_A_w, L1_ffn_B_w):
    x = np.asarray(L0_attn_hidden_states, dtype=np.float32)
    B, S, _ = x.shape
    w = dict(L0_attn_Wq=L0_attn_Wq, L0_attn_Wk=L0_attn_Wk, L0_attn_Wv=L0_attn_Wv,
             L0_attn_Wo=L0_attn_Wo, L0_attn_bo=L0_attn_bo,
             L0_ffn_weight=L0_ffn_weight, L0_ffn_bias=L0_ffn_bias,
             L1_attn_Wq=L1_attn_Wq, L1_attn_bq=L1_attn_bq,
             L1_attn_Wk=L1_attn_Wk, L1_attn_bk=L1_attn_bk,
             L1_attn_Wv=L1_attn_Wv, L1_attn_bv=L1_attn_bv,
             L1_attn_Wo=L1_attn_Wo, L1_attn_bo=L1_attn_bo,
             L1_ffn_weight=L1_ffn_weight, L1_ffn_gate_w=L1_ffn_gate_w,
             L1_ffn_A_w=L1_ffn_A_w, L1_ffn_B_w=L1_ffn_B_w)
    nc = _get_nc(S, False)
    in_maps = make_in_maps(x, w, S)
    res = run_bass_kernel_spmd(nc, in_maps, core_ids=list(range(8)))
    outf = np.empty((B, S // 128, 128, D), np.float32)
    for c in range(8):
        outf[c // 2, c % 2::2] = res.results[c]["out"]
    return outf.reshape(B, S, D)
